# version 02 (final)
# baseline (speedup 1.0000x reference)
import contextlib
import numpy as np

import concourse.bass as bass
import concourse.mybir as mybir
from concourse.bass_utils import run_bass_kernel_spmd

F32 = mybir.dt.float32
BF16 = mybir.dt.bfloat16
AF = mybir.ActivationFunctionType
ALU = mybir.AluOpType
AX = mybir.AxisListType

D = 1024
NC_CH = 8
FF = 4096
NFC = 32
NE = 8
H = 16
HD = 64
LNB = 13.815510557964274

W1T = 256
W2T = 512


class Sem:
    def __init__(self, ctx, nc, name):
        self.h = ctx.enter_context(nc.semaphore(name))
        self.n = 0

    def inc(self, instr, k=1):
        instr.then_inc(self.h, k)
        self.n += k
        return instr


class Bank:
    def __init__(self, ps):
        self.ps = ps
        self.readers = []
        self.ready = None

    def guard_write(self, tensor):
        for (s, n) in self.readers:
            tensor.wait_ge(s.h, n)
        self.readers = []

    def wait_ready(self, eng):
        s, n = self.ready
        eng.wait_ge(s.h, n)

    def add_reader(self, sem):
        self.readers.append((sem, sem.n))


_NOTRACK = {"wait_ge", "wait_op", "nop", "engine_nop", "maybe_wait_ge"}


class EngTrack:

    def __init__(self, eng):
        object.__setattr__(self, "_e", eng)
        object.__setattr__(self, "last", None)

    def __getattr__(self, name):
        f = getattr(self._e, name)
        if callable(f) and name not in _NOTRACK and not name.startswith("_"):
            def wrap(*a, **k):
                r = f(*a, **k)
                if r is not None and hasattr(r, "then_inc"):
                    object.__setattr__(self, "last", r)
                return r
            return wrap
        return f


class KB:

    def __init__(self, nc, ctx, block):
        self.nc = nc
        self.ctx = ctx
        self.block = block

    def V(self, fn): self.block.vector(lambda e: fn(EngTrack(e)))
    def A(self, fn): self.block.scalar(lambda e: fn(EngTrack(e)))
    def T(self, fn): self.block.tensor(lambda e: fn(EngTrack(e)))
    def G(self, fn): self.block.gpsimd(lambda e: fn(EngTrack(e)))
    def Y(self, fn): self.block.sync(lambda e: fn(EngTrack(e)))


def build_nc(n_main=4096, n_pre=4096, tap=None, phase2=True, max_tiles=None):
    nc = bass.Bass("TRN2", target_bir_lowering=False)
    NT1 = n_main // W1T
    NTP = n_pre // W1T
    NT2 = n_main // W2T

    dt = nc.dram_tensor
    x_main = dt("x_main", [D, n_main], F32, kind="ExternalInput").ap()
    x_pre = dt("x_pre", [D, n_pre], F32, kind="ExternalInput").ap()
    cos_main = dt("cos_main", [D, n_main], F32, kind="ExternalInput").ap()
    cos_pre = dt("cos_pre", [D, n_pre], F32, kind="ExternalInput").ap()
    iota_main = dt("iota_main", [1, n_main], F32, kind="ExternalInput").ap()
    cmask_in = dt("cmask", [128, 1], F32, kind="ExternalInput").ap()
    vecs_in = dt("vecs", [128, 96], F32, kind="ExternalInput").ap()
    convw_in = dt("convw", [128, 32], F32, kind="ExternalInput").ap()
    gcol_in = dt("gcol", [128, 8], F32, kind="ExternalInput").ap()
    grow_in = dt("grow", [1, D], F32, kind="ExternalInput").ap()
    hsel_in = dt("hsel", [128, 128], F32, kind="ExternalInput").ap()
    hselT_in = dt("hselT", [16, D], F32, kind="ExternalInput").ap()
    ident_in = dt("ident", [128, 128], F32, kind="ExternalInput").ap()
    esel_in = dt("esel", [8, D], F32, kind="ExternalInput").ap()
    b2_in = dt("b2", [8, D], F32, kind="ExternalInput").ap()
    b1_in = dt("b1", [128, 256], F32, kind="ExternalInput").ap()
    br_in = dt("br", [1, 8], F32, kind="ExternalInput").ap()
    wrT_in = dt("wrT", [8, 128, 8], F32, kind="ExternalInput").ap()
    qkT_in = dt("qkT", [8, 8, 128, 128], F32, kind="ExternalInput").ap()
    vT_in = dt("vT", [8, 8, 128, 128], F32, kind="ExternalInput").ap()
    gT_in = dt("gT", [8, 8, 128, 128], F32, kind="ExternalInput").ap()
    oT_in = dt("oT", [8, 8, 128, 128], F32, kind="ExternalInput").ap()
    gateT_in = dt("gateT", [32, 8, 128, 128], F32, kind="ExternalInput").ap()
    upT_in = dt("upT", [32, 8, 128, 128], F32, kind="ExternalInput").ap()
    downT_in = dt("downT", [8, 32, 128, 128], F32, kind="ExternalInput").ap()
    w1T_in = dt("w1T", [8, 32, 8, 128, 128], F32, kind="ExternalInput").ap()
    w2T_in = dt("w2T", [8, 32, 128, 1024], F32, kind="ExternalInput").ap()
    qkT_s = dt("qkT_s", [8, 8, 128, 128], BF16, kind="Internal").ap()
    vT_s = dt("vT_s", [8, 8, 128, 128], BF16, kind="Internal").ap()
    gT_s = dt("gT_s", [8, 8, 128, 128], BF16, kind="Internal").ap()
    oT_s = dt("oT_s", [8, 8, 128, 128], BF16, kind="Internal").ap()
    gateT_s = dt("gateT_s", [32, 8, 128, 128], BF16, kind="Internal").ap()
    upT_s = dt("upT_s", [32, 8, 128, 128], BF16, kind="Internal").ap()
    downT_s = dt("downT_s", [8, 32, 128, 128], BF16, kind="Internal").ap()
    w1T_s = dt("w1T_s", [8, 32, 8, 128, 128], BF16, kind="Internal").ap()
    w2T_s = dt("w2T_s", [8, 32, 128, 1024], BF16, kind="Internal").ap()
    out_t = dt("out_t", [D, n_main], F32, kind="ExternalOutput").ap()

    with contextlib.ExitStack() as octx:
        E = octx.enter_context
        block = E(nc.Block())
        K = KB(nc, octx, block)
        wq = Sem(octx, nc, "wq")
        aq = [Sem(octx, nc, f"aq{i}") for i in range(2)]
        cq = [Sem(octx, nc, f"cq{i}") for i in range(2)]
        xq = [Sem(octx, nc, f"xq{i}") for i in range(2)]
        x2q = Sem(octx, nc, "x2q")
        w2q = Sem(octx, nc, "w2q")
        oq0 = Sem(octx, nc, "oq0")
        oq1 = Sem(octx, nc, "oq1")
        oq2 = Sem(octx, nc, "oq2")
        sV = Sem(octx, nc, "sV")
        sS = Sem(octx, nc, "sS")
        sT = Sem(octx, nc, "sT")
        sG = Sem(octx, nc, "sG")
        STRM = {}
        for snm in ("qk", "v", "g", "o", "gate", "up", "down", "w1"):
            STRM[snm] = [Sem(octx, nc, f"st_{snm}{i}") for i in range(2)]

        vecs = E(nc.sbuf_tensor("vecs_sb", [128, 96], F32))
        convw = E(nc.sbuf_tensor("convw_sb", [128, 32], F32))
        gcol = E(nc.sbuf_tensor("gcol_sb", [128, 8], F32))
        gcinv = E(nc.sbuf_tensor("gcinv_sb", [128, 8], F32))
        cmask = E(nc.sbuf_tensor("cmask_sb", [128, 1], F32))
        ones_col = E(nc.sbuf_tensor("ones_col", [128, 1], F32))
        ones_row = E(nc.sbuf_tensor("ones_row", [1, 128], F32))
        cst_misc = E(nc.sbuf_tensor("cst_misc", [128, 2], F32))
        nc.const_aps.register(F32, 1e-5, cst_misc[:, 0:1])
        nc.const_aps.register(F32, LNB, cst_misc[:, 1:2])
        midctx = octx.enter_context(contextlib.ExitStack())
        Em = midctx.enter_context
        lngrow = Em(nc.sbuf_tensor("lngrow_sb", [1, D], F32))
        hsel = Em(nc.sbuf_tensor("hsel_sb", [128, 128], F32))
        hselT = Em(nc.sbuf_tensor("hselT_sb", [16, D], F32))
        grow_sb = Em(nc.sbuf_tensor("grow_sb", [1, D], F32))

        VEC = {n: i for i, n in enumerate(
            ["ln1w", "ln1b", "convb", "gb", "onw", "onb", "ln2w", "ln2b",
             "g1", "g2", "fgam", "fbeta"])}

        def vcol(name):
            i = VEC[name]
            return lambda c: vecs[:, i * 8 + c:i * 8 + c + 1]

        ln1w, ln1b = vcol("ln1w"), vcol("ln1b")
        convb, gb = vcol("convb"), vcol("gb")
        onw, onb = vcol("onw"), vcol("onb")
        ln2w, ln2b = vcol("ln2w"), vcol("ln2b")
        g1c, g2c = vcol("g1"), vcol("g2")
        fgamc, fbetac = vcol("fgam"), vcol("fbeta")

        def Yl(sync):
            for dst, src in [(vecs[:], vecs_in), (convw[:], convw_in),
                             (gcol[:], gcol_in), (grow_sb[:], grow_in),
                             (hsel[:], hsel_in), (hselT[:], hselT_in),
                             (cmask[:], cmask_in)]:
                wq.inc(sync.dma_start(out=dst, in_=src), 16)
        K.Y(Yl)
        n_const_ld = wq.n

        def Vinit(v):
            v.wait_ge(wq.h, n_const_ld)
            v.memset(ones_col[:], 1.0)
            v.memset(ones_row[:], 1.0)
            v.memset(cst_misc[:, 0:1], 1e-5)
            v.memset(cst_misc[:, 1:2], LNB)
            sV.inc(v.last)
        K.V(Vinit)
        vinit_n = sV.n

        def Ainit(a):
            a.wait_ge(wq.h, n_const_ld)
            a.wait_ge(sV.h, vinit_n)
            sS.inc(a.activation(out=lngrow[:], in_=grow_sb[:], func=AF.Ln))
        K.A(Ainit)

        cst_f = [Em(nc.sbuf_tensor(f"cst_f{i}", [128, 512], F32)) for i in range(2)]
        cst_b = [Em(nc.sbuf_tensor(f"cst_b{i}", [128, 512], BF16)) for i in range(2)]
        cast_jobs_a = [(vT_in, vT_s), (gT_in, gT_s), (oT_in, oT_s),
                       (gateT_in, gateT_s), (upT_in, upT_s), (downT_in, downT_s), (qkT_in, qkT_s)]
        cast_jobs_b = [(w1T_in, w1T_s), (w2T_in, w2T_s)]
        cast_state = {"k": 0, "free": [None, None], "done": [None, None]}

        def cast_blocks(jobs):
            blocks = []
            for src, dst in jobs:
                tot = int(np.prod(src.shape))
                nblk = tot // (128 * 512)
                sflat = src.flatten().rearrange("(n p m) -> n p m", p=128, m=512)
                dflat = dst.flatten().rearrange("(n p m) -> n p m", p=128, m=512)
                for i in range(nblk):
                    blocks.append((sflat[i], dflat[i]))
            return blocks

        def emit_cast(blocks):
            for (s, d) in blocks:
                k = cast_state["k"]
                par = k % 2
                fr = cast_state["free"][par]
                dn = cast_state["done"][par]

                def fy(sync, s=s, par=par, fr=fr):
                    if fr is not None:
                        sync.wait_ge(fr[0].h, fr[1])
                    cq[par].inc(sync.dma_start(out=cst_f[par][:], in_=s), 16)
                K.Y(fy)
                wq_after = cq[par].n

                def fg(g, par=par, wq_after=wq_after, dn=dn):
                    g.wait_ge(cq[par].h, wq_after)
                    if dn is not None:
                        g.wait_ge(dn[0].h, dn[1])
                    sG.inc(g.tensor_copy(out=cst_b[par][:], in_=cst_f[par][:]))
                K.G(fg)
                cast_state["free"][par] = (sG, sG.n)
                sg_after = sG.n

                def fa(a, d=d, par=par, sg_after=sg_after):
                    a.wait_ge(sG.h, sg_after)
                    aq[par].inc(a.dma_start(out=d, in_=cst_b[par][:]), 16)
                K.A(fa)
                cast_state["done"][par] = (aq[par], aq[par].n)
                cast_state["k"] += 1

        emit_cast(cast_blocks(cast_jobs_a))
        cast_a_done = (aq[0].n, aq[1].n)

        with contextlib.ExitStack() as p1ctx:
            E1 = p1ctx.enter_context
            Wt = W1T
            xb = [E1(nc.sbuf_tensor(f"xb{p}", [128, 8 * Wt], F32)) for p in range(2)]
            iota_t = [E1(nc.sbuf_tensor(f"iota_t{p}", [1, Wt], F32)) for p in range(2)]
            cosb = [E1(nc.sbuf_tensor(f"cosb{p}", [128, 8 * Wt], F32)) for p in range(2)]
            xnb = E1(nc.sbuf_tensor("xnb", [128, 8 * Wt], F32))
            h1b = E1(nc.sbuf_tensor("h1b", [128, 8 * (Wt + 3)], F32))
            xcbf = E1(nc.sbuf_tensor("xcbf", [128, 8 * Wt], BF16))
            qkb = E1(nc.sbuf_tensor("qkb", [128, 8 * Wt], F32))
            kb = E1(nc.sbuf_tensor("kb", [128, 8 * (Wt + 1)], F32))
            vb = E1(nc.sbuf_tensor("vb", [128, 8 * Wt], F32))
            gbuf = E1(nc.sbuf_tensor("gbuf", [128, 8 * Wt], F32))
            obf = E1(nc.sbuf_tensor("obf", [128, 8 * Wt], BF16))
            mbf = E1(nc.sbuf_tensor("mbf", [128, 32 * Wt], BF16))
            sq = E1(nc.sbuf_tensor("sq", [128, 2 * Wt], F32))
            sg_t = E1(nc.sbuf_tensor("sg_t", [128, 2 * Wt], F32))
            att2 = E1(nc.sbuf_tensor("att2", [128, 2 * Wt], F32))
            rows = E1(nc.sbuf_tensor("rows", [1, 10 * Wt], F32))
            r16 = E1(nc.sbuf_tensor("r16", [16, 2 * Wt], F32))
            mcarry = E1(nc.sbuf_tensor("mcarry", [128, 8], F32))
            gt = E1(nc.sbuf_tensor("gt", [128, 8 * Wt], F32))
            xct = E1(nc.sbuf_tensor("xct", [128, Wt], F32))
            wqk = [E1(nc.sbuf_tensor(f"wqk{p}", [128, 1024], BF16)) for p in range(2)]
            wv = [E1(nc.sbuf_tensor(f"wv{p}", [128, 1024], BF16)) for p in range(2)]
            wg = [E1(nc.sbuf_tensor(f"wg{p}", [128, 1024], BF16)) for p in range(2)]
            wo = [E1(nc.sbuf_tensor(f"wo{p}", [128, 1024], BF16)) for p in range(2)]
            wgate = [E1(nc.sbuf_tensor(f"wgate{p}", [128, 1024], BF16)) for p in range(2)]
            wup = [E1(nc.sbuf_tensor(f"wup{p}", [128, 1024], BF16)) for p in range(2)]
            wdown = [E1(nc.sbuf_tensor(f"wdown{p}", [128, 4096], BF16)) for p in range(2)]
            P0 = Bank(E1(nc.psum_tensor("P0", [128, 512], F32)))
            P1 = Bank(E1(nc.psum_tensor("P1", [128, 512], F32)))
            PS = Bank(E1(nc.psum_tensor("PS", [128, 512], F32)))
            PB1 = Bank(E1(nc.psum_tensor("PB1", [128, 512], F32)))
            PB2 = Bank(E1(nc.psum_tensor("PB2", [128, 512], F32)))
            PB3 = Bank(E1(nc.psum_tensor("PB3", [128, 512], F32)))
            PH = Bank(E1(nc.psum_tensor("PH", [128, 512], F32)))
            PEx = Bank(E1(nc.psum_tensor("PEx", [128, 512], F32)))
            PP = [P0, P1]

            def Vgt2(v):
                v.wait_ge(wq.h, n_const_ld)
                v.memset(xct[:, 0:Wt], 1.0)
                v.reciprocal(out=gcinv[:], in_=gcol[:])
                sV.inc(v.last)
                v.wait_ge(sV.h, sV.n)
                for c in range(8):
                    v.tensor_scalar(out=gt[:, c * Wt:(c + 1) * Wt],
                                    in0=xct[:, 0:Wt], scalar1=gcinv[:, c:c + 1],
                                    scalar2=None, op0=ALU.mult)
                sV.inc(v.last)
            K.V(Vgt2)

            stream_free = {}

            def wload(sync, name, bufs, par, src3, nk, m):
                fr = stream_free.get((name, par))
                if fr is not None:
                    sync.wait_ge(fr[0].h, fr[1])
                dst3 = bufs[par][:].rearrange("p (k m) -> p k m", k=nk)
                sm = STRM[name][par]
                sm.inc(sync.dma_start(out=dst3, in_=src3), 16)
                return (sm, sm.n)

            sq_free = [None, None]

            def ln_stats(src, width, wait_src):
                def ft0(t):
                    PS.guard_write(t)
                    if wait_src is not None:
                        t.wait_ge(wait_src[0].h, wait_src[1])
                    mm = None
                    for c in range(8):
                        mm = t.matmul(out=PS.ps[0:1, 0:width], lhsT=ones_col[:], rhs=src(c),
                                      start=(c == 0), stop=(c == 7))
                    sT.inc(mm)
                K.T(ft0)
                for c in range(8):
                    par = c % 2
                    fr = sq_free[par]

                    def fa(a, c=c, par=par, fr=fr):
                        if c == 0 and wait_src is not None:
                            a.wait_ge(wait_src[0].h, wait_src[1])
                        if fr is not None:
                            a.wait_ge(fr[0].h, fr[1])
                        sS.inc(a.activation(out=sq[:, par * Wt:par * Wt + width],
                                            in_=src(c), func=AF.Square))
                    K.A(fa)
                    ss_after = sS.n

                    def ft(t, c=c, par=par, ss_after=ss_after):
                        t.wait_ge(sS.h, ss_after)
                        mm = t.matmul(out=PS.ps[32:33, 0:width], lhsT=ones_col[:],
                                      rhs=sq[:, par * Wt:par * Wt + width],
                                      start=(c == 0), stop=(c == 7))
                        sT.inc(mm)
                    K.T(ft)
                    sq_free[par] = (sT, sT.n)
                PS.ready = (sT, sT.n)

            def R(i, width):
                return rows[0:1, i * Wt:i * Wt + width]

            def ln_rows(width, with_beta):
                def fv(v, width=width):
                    PS.wait_ready(v)
                    v.tensor_scalar(out=R(0, width), in0=PS.ps[0:1, 0:width],
                                    scalar1=1.0 / 1024.0, scalar2=None, op0=ALU.mult)
                    v.tensor_scalar(out=R(1, width), in0=PS.ps[32:33, 0:width],
                                    scalar1=1.0 / 1024.0, scalar2=None, op0=ALU.mult)
                    sV.inc(v.last)
                    v.wait_ge(sV.h, sV.n)
                    v.tensor_tensor(out=R(2, width), in0=R(0, width), in1=R(0, width), op=ALU.mult)
                    sV.inc(v.last)
                    v.wait_ge(sV.h, sV.n)
                    sV.inc(v.tensor_tensor(out=R(2, width), in0=R(1, width), in1=R(2, width), op=ALU.subtract))
                K.V(fv)
                PS.add_reader(sV)
                sv0 = sV.n

                def fa(a, width=width, with_beta=with_beta):
                    a.wait_ge(sV.h, sv0)
                    sS.inc(a.activation(out=R(3, width), in_=R(2, width), func=AF.Ln, bias=1e-5))
                    a.wait_ge(sS.h, sS.n)
                    sS.inc(a.activation(out=R(4, width), in_=R(3, width), func=AF.Exp, scale=-0.5))
                    if with_beta:
                        a.wait_ge(sS.h, sS.n)
                        sS.inc(a.activation(out=R(5, width), in_=R(3, width), func=AF.Exp, scale=-1.0))
                K.A(fa)
                ss0 = sS.n

                def fv2(v, width=width, with_beta=with_beta):
                    v.wait_ge(sS.h, ss0)
                    sV.inc(v.tensor_tensor(out=R(7, width), in0=R(0, width), in1=R(4, width), op=ALU.mult))
                    if with_beta:
                        v.tensor_tensor(out=R(5, width), in0=R(2, width), in1=R(5, width), op=ALU.mult)
                        sV.inc(v.last)
                K.V(fv2)
                sv1 = sV.n
                if with_beta:
                    def fa2(a, width=width):
                        a.wait_ge(sV.h, sv1)
                        sS.inc(a.activation(out=R(5, width), in_=R(5, width), func=AF.Ln, bias=1e-5))
                        a.wait_ge(sS.h, sS.n)
                        sS.inc(a.activation(out=R(6, width), in_=R(5, width), func=AF.Exp, scale=-0.5))
                    K.A(fa2)
                    ss1 = sS.n

                    def fv3(v, width=width):
                        v.wait_ge(sS.h, ss1)
                        v.tensor_tensor(out=R(8, width), in0=R(4, width), in1=R(6, width), op=ALU.mult)
                        sV.inc(v.last)
                        v.wait_ge(sV.h, sV.n)
                        sV.inc(v.tensor_tensor(out=R(9, width), in0=R(7, width), in1=R(6, width), op=ALU.mult))
                    K.V(fv3)

            def bcast_rows(width, pairs):
                rows_ready = (sV, sV.n)
                by_bank = {}
                for (bk, off, ri) in pairs:
                    by_bank.setdefault(id(bk), (bk, []))[1].append((off, ri))
                for _, (bk, lst) in by_bank.items():
                    def ft(t, bk=bk, lst=lst):
                        bk.guard_write(t)
                        t.wait_ge(rows_ready[0].h, rows_ready[1])
                        mm = None
                        for (off, ri) in lst:
                            mm = t.matmul(out=bk.ps[:, off:off + width],
                                          lhsT=ones_row[:], rhs=R(ri, width),
                                          start=True, stop=True)
                        sT.inc(mm)
                        bk.ready = (sT, sT.n)
                    K.T(ft)

            xb_free = [None, None]
            tile_list = [("pre", i) for i in range(NTP)] + [("main", i) for i in range(NT1)]
            def Yp0(sync):
                sync.wait_ge(aq[0].h, cast_a_done[0])
                sync.wait_ge(aq[1].h, cast_a_done[1])
                kind, i = tile_list[0]
                xs = (x_pre if kind == "pre" else x_main)
                cs = (cos_pre if kind == "pre" else cos_main)
                src_x = xs.rearrange("(c p) t -> c p t", p=128)
                src_c = cs.rearrange("(c p) t -> c p t", p=128)
                n = n_pre if kind == "pre" else n_main
                for c in range(8):
                    xq[0].inc(sync.dma_start(out=xb[0][:, c * Wt:(c + 1) * Wt],
                                             in_=src_x[c][:, i * Wt:(i + 1) * Wt]), 16)
                    xq[0].inc(sync.dma_start(out=cosb[0][:, c * Wt:(c + 1) * Wt],
                                             in_=src_c[c][:, i * Wt:(i + 1) * Wt]), 16)
                if kind == "main":
                    xq[0].inc(sync.dma_start(out=iota_t[0][:],
                                             in_=iota_main[0:1, i * Wt:(i + 1) * Wt]), 16)
            K.Y(Yp0)
            xcos_ready = [None, None]
            xcos_ready[0] = (xq[0], xq[0].n)

            cast_b_blocks = cast_blocks(cast_jobs_b)
            ncast_per_tile = (len(cast_b_blocks) + len(tile_list) - 1) // len(tile_list)
            cast_b_pos = [0]

            def emit_tile(ti):
                kind, i = tile_list[ti]
                pre = (kind == "pre")
                par = ti % 2
                first_main = (kind == "main" and i == 0)
                lo = cast_b_pos[0]
                hi = min(lo + ncast_per_tile, len(cast_b_blocks))
                if hi > lo:
                    emit_cast(cast_b_blocks[lo:hi])
                    cast_b_pos[0] = hi
                if ti + 1 < len(tile_list):
                    kind2, i2 = tile_list[ti + 1]
                    par2 = (ti + 1) % 2
                    fr = xb_free[par2]

                    def Ypre(sync, kind2=kind2, i2=i2, par2=par2, fr=fr):
                        if fr is not None:
                            sync.wait_ge(fr[0].h, fr[1])
                        xs = (x_pre if kind2 == "pre" else x_main)
                        cs = (cos_pre if kind2 == "pre" else cos_main)
                        n = n_pre if kind2 == "pre" else n_main
                        src_x = xs.rearrange("(c p) t -> c p t", p=128)
                        src_c = cs.rearrange("(c p) t -> c p t", p=128)
                        for c in range(8):
                            xq[par2].inc(sync.dma_start(out=xb[par2][:, c * Wt:(c + 1) * Wt],
                                                        in_=src_x[c][:, i2 * Wt:(i2 + 1) * Wt]), 16)
                            xq[par2].inc(sync.dma_start(out=cosb[par2][:, c * Wt:(c + 1) * Wt],
                                                        in_=src_c[c][:, i2 * Wt:(i2 + 1) * Wt]), 16)
                        if kind2 == "main":
                            xq[par2].inc(sync.dma_start(out=iota_t[par2][:],
                                                        in_=iota_main[0:1, i2 * Wt:(i2 + 1) * Wt]), 16)
                    K.Y(Ypre)
                    xcos_ready[(ti + 1) % 2] = (xq[par2], xq[par2].n)

                def tapdump(bufsl):
                    def As(a):
                        a.wait_ge(sV.h, sV.n)
                        a.wait_ge(sS.h, sS.n)
                        dst = out_t.rearrange("(c p) t -> c p t", p=128)
                        for c in range(8):
                            oq2.inc(a.dma_start(out=dst[c][:, i * Wt:(i + 1) * Wt],
                                                in_=bufsl(c)), 16)
                    K.A(As)

                X = lambda c: xb[par][:, c * Wt:(c + 1) * Wt]
                COS = lambda c: cosb[par][:, c * Wt:(c + 1) * Wt]
                XN = lambda c: xnb[:, c * Wt:(c + 1) * Wt]
                H1 = lambda c, a, b: h1b[:, c * (Wt + 3) + a: c * (Wt + 3) + b]
                QK = lambda c: qkb[:, c * Wt:(c + 1) * Wt]
                KBf = lambda c, a, b: kb[:, c * (Wt + 1) + a: c * (Wt + 1) + b]
                VB = lambda c: vb[:, c * Wt:(c + 1) * Wt]
                GB = lambda c: gbuf[:, c * Wt:(c + 1) * Wt]
                OBF = lambda c: obf[:, c * Wt:(c + 1) * Wt]
                GT = lambda c: gt[:, c * Wt:(c + 1) * Wt]

                xrdy = xcos_ready[par]

                def Vcar(v):
                    v.wait_ge(sV.h, sV.n)
                    if ti == 0:
                        for c in range(8):
                            v.memset(H1(c, 0, 3), 0.0)
                            v.memset(KBf(c, 0, 1), 0.0)
                        v.memset(mcarry[:], 0.0)
                    elif first_main:
                        for c in range(8):
                            v.tensor_scalar(out=H1(c, 0, 3), in0=H1(c, Wt, Wt + 3),
                                            scalar1=cmask[:], scalar2=None, op0=ALU.mult)
                            v.tensor_scalar(out=KBf(c, 0, 1), in0=KBf(c, Wt, Wt + 1),
                                            scalar1=cmask[:], scalar2=None, op0=ALU.mult)
                        v.tensor_scalar(out=mcarry[:], in0=mcarry[:],
                                        scalar1=cmask[:], scalar2=None, op0=ALU.mult)
                    else:
                        for c in range(8):
                            v.tensor_copy(out=H1(c, 0, 3), in_=H1(c, Wt, Wt + 3))
                            v.tensor_copy(out=KBf(c, 0, 1), in_=KBf(c, Wt, Wt + 1))
                    sV.inc(v.last)
                    v.wait_ge(sV.h, sV.n)
                K.V(Vcar)

                ln_stats(X, Wt, xrdy)
                ln_rows(Wt, with_beta=True)
                bcast_rows(Wt, [(PB1, 0, 4), (PB1, Wt, 8), (PB2, 0, 7), (PB2, Wt, 9)])

                def Vxn(v):
                    PB1.wait_ready(v)
                    PB2.wait_ready(v)
                    for c in range(8):
                        if not pre:
                            v.tensor_tensor(out=XN(c), in0=X(c), in1=PB1.ps[:, 0:Wt], op=ALU.mult)
                        v.tensor_tensor(out=H1(c, 3, Wt + 3), in0=X(c), in1=PB1.ps[:, Wt:2 * Wt], op=ALU.mult)
                    sV.inc(v.last)
                    v.wait_ge(sV.h, sV.n)
                    for c in range(8):
                        if not pre:
                            v.tensor_tensor(out=XN(c), in0=XN(c), in1=PB2.ps[:, 0:Wt], op=ALU.subtract)
                        v.tensor_tensor(out=H1(c, 3, Wt + 3), in0=H1(c, 3, Wt + 3),
                                        in1=PB2.ps[:, Wt:2 * Wt], op=ALU.subtract)
                    sV.inc(v.last)
                    v.wait_ge(sV.h, sV.n)
                    for c in range(8):
                        v.tensor_scalar(out=H1(c, 3, Wt + 3), in0=H1(c, 3, Wt + 3),
                                        scalar1=ln1w(c), scalar2=ln1b(c), op0=ALU.mult, op1=ALU.add)
                    sV.inc(v.last)
                K.V(Vxn)
                PB1.add_reader(sV)
                PB2.add_reader(sV)
                if tap == "xn" and not pre:
                    tapdump(XN)
                    xb_free[par] = (oq2, oq2.n)
                    return
                if tap == "h1" and not pre:
                    tapdump(lambda c: H1(c, 3, Wt + 3))
                    xb_free[par] = (oq2, oq2.n)
                    return

                def Vconv(v):
                    v.wait_ge(sV.h, sV.n)
                    for c in range(8):
                        v.tensor_scalar(out=xct[:, 0:Wt], in0=H1(c, 0, Wt),
                                        scalar1=convw[:, c * 4:c * 4 + 1], scalar2=convb(c),
                                        op0=ALU.mult, op1=ALU.add)
                        for j in (1, 2):
                            v.scalar_tensor_tensor(out=xct[:, 0:Wt], in0=H1(c, j, Wt + j),
                                                   scalar=convw[:, c * 4 + j:c * 4 + j + 1],
                                                   in1=xct[:, 0:Wt], op0=ALU.mult, op1=ALU.add)
                        v.scalar_tensor_tensor(out=xcbf[:, c * Wt:(c + 1) * Wt], in0=H1(c, 3, Wt + 3),
                                               scalar=convw[:, c * 4 + 3:c * 4 + 4],
                                               in1=xct[:, 0:Wt], op0=ALU.mult, op1=ALU.add)
                    sV.inc(v.last)
                K.V(Vconv)
                xc_ready = sV.n

                for proj, wbufs, wsrc, evac in (
                        ("qk", wqk, qkT_s, "copy"),
                        ("v", wv, vT_s, "copy"),
                        ("g", wg, gT_s, "sigmoid")):
                    for oc in range(8):
                        wpar = oc % 2

                        wn_box = []

                        def Yw(sync, wbufs=wbufs, wsrc=wsrc, oc=oc, wpar=wpar, proj=proj):
                            pass
                            wn_box.append(wload(sync, proj, wbufs, wpar,
                                  wsrc[oc].rearrange("kc p m -> p kc m"), 8, 128))
                        K.Y(Yw)
                        wn = wn_box[0]
                        bank = PP[(oc + (0 if proj == "qk" else (8 if proj == "v" else 16))) % 2]

                        def ft(t, oc=oc, wpar=wpar, wn=wn, bank=bank, wbufs=wbufs, proj=proj):
                            bank.guard_write(t)
                            t.wait_ge(wn[0].h, wn[1])
                            if oc == 0 and proj == "qk":
                                t.wait_ge(sV.h, xc_ready)
                            mm = None
                            for kc in range(8):
                                mm = t.matmul(out=bank.ps[:, 0:Wt],
                                              lhsT=wbufs[wpar][:, kc * 128:(kc + 1) * 128],
                                              rhs=xcbf[:, kc * Wt:(kc + 1) * Wt],
                                              start=(kc == 0), stop=(kc == 7))
                            sT.inc(mm)
                            bank.ready = (sT, sT.n)
                        K.T(ft)
                        stream_free[(proj, wpar)] = (sT, sT.n)

                        def fa(a, oc=oc, bank=bank, evac=evac, proj=proj):
                            bank.wait_ready(a)
                            if evac == "copy":
                                dst = QK(oc) if proj == "qk" else VB(oc)
                                sS.inc(a.activation(out=dst, in_=bank.ps[:, 0:Wt], func=AF.Identity))
                            else:
                                sS.inc(a.activation(out=GB(oc), in_=bank.ps[:, 0:Wt],
                                                    func=AF.Sigmoid, bias=gb(oc), scale=1.0))
                        K.A(fa)
                        bank.add_reader(sS)

                if tap == "qk" and not pre:
                    tapdump(QK)
                    xb_free[par] = (oq2, oq2.n)
                    return
                for c in range(8):
                    spar = c % 2
                    fr = sq_free[spar]

                    def fa(a, c=c, spar=spar, fr=fr):
                        if fr is not None:
                            a.wait_ge(fr[0].h, fr[1])
                        a.wait_ge(sS.h, sS.n)
                        sS.inc(a.activation(out=sq[:, spar * Wt:(spar + 1) * Wt],
                                            in_=QK(c), func=AF.Square))
                    K.A(fa)
                    ssn = sS.n

                    def ft(t, c=c, spar=spar, ssn=ssn):
                        if c == 0:
                            PH.guard_write(t)
                        t.wait_ge(sS.h, ssn)
                        mm = t.matmul(out=PH.ps[0:16, 0:Wt], lhsT=hsel[:, c * 16:(c + 1) * 16],
                                      rhs=sq[:, spar * Wt:(spar + 1) * Wt],
                                      start=(c == 0), stop=(c == 7))
                        sT.inc(mm)
                    K.T(ft)
                    sq_free[spar] = (sT, sT.n)
                PH.ready = (sT, sT.n)

                def Vrn(v):
                    PH.wait_ready(v)
                    sV.inc(v.tensor_scalar(out=r16[0:16, 0:Wt], in0=PH.ps[0:16, 0:Wt],
                                           scalar1=1e-24, scalar2=None, op0=ALU.max))
                K.V(Vrn)
                PH.add_reader(sV)
                svrn = sV.n

                def Arn(a):
                    a.wait_ge(sV.h, svrn)
                    sS.inc(a.activation(out=r16[0:16, Wt:2 * Wt], in_=r16[0:16, 0:Wt], func=AF.Ln))
                    a.wait_ge(sS.h, sS.n)
                    sS.inc(a.activation(out=r16[0:16, Wt:2 * Wt], in_=r16[0:16, Wt:2 * Wt],
                                        func=AF.Exp, scale=-0.5))
                K.A(Arn)
                ssrn = sS.n

                for c in range(8):
                    bank = PP[c % 2] if False else (PB1 if c % 2 == 0 else PB2)

                    def ft(t, c=c, bank=bank, ssrn=ssrn):
                        bank.guard_write(t)
                        t.wait_ge(sS.h, ssrn)
                        mm = t.matmul(out=bank.ps[:, 0:Wt], lhsT=hselT[:, c * 128:(c + 1) * 128],
                                      rhs=r16[0:16, Wt:2 * Wt], start=True, stop=True)
                        sT.inc(mm)
                        bank.ready = (sT, sT.n)
                    K.T(ft)

                    def fv(v, c=c, bank=bank):
                        bank.wait_ready(v)
                        sV.inc(v.tensor_tensor(out=KBf(c, 1, Wt + 1), in0=QK(c),
                                               in1=bank.ps[:, 0:Wt], op=ALU.mult))
                    K.V(fv)
                    bank.add_reader(sV)

                if tap == "k" and not pre:
                    tapdump(lambda c: KBf(c, 1, Wt + 1))
                    xb_free[par] = (oq2, oq2.n)
                    return
                if tap == "g" and not pre:
                    tapdump(GB)
                    xb_free[par] = (oq2, oq2.n)
                    return
                def Vst(v):
                    v.wait_ge(xrdy[0].h, xrdy[1])
                    v.wait_ge(sS.h, sS.n)
                    v.wait_ge(sV.h, sV.n)
                    for c in range(8):
                        v.tensor_tensor(out=COS(c), in0=COS(c), in1=KBf(c, 0, Wt), op=ALU.add)
                        v.tensor_tensor(out=VB(c), in0=VB(c), in1=GB(c), op=ALU.mult)
                    sV.inc(v.last)
                    v.wait_ge(sV.h, sV.n)
                    for c in range(8):
                        v.tensor_tensor(out=COS(c), in0=COS(c), in1=VB(c), op=ALU.mult)
                    sV.inc(v.last)
                    v.wait_ge(sV.h, sV.n)
                    for c in range(8):
                        v.tensor_tensor_scan(out=VB(c), data0=GT(c), data1=COS(c),
                                             initial=mcarry[:, c:c + 1], op0=ALU.mult, op1=ALU.add)
                    sV.inc(v.last)
                    v.wait_ge(sV.h, sV.n)
                    for c in range(8):
                        v.tensor_copy(out=mcarry[:, c:c + 1], in_=vb[:, c * Wt + Wt - 1: c * Wt + Wt])
                    sV.inc(v.last)
                    v.wait_ge(sV.h, sV.n)
                K.V(Vst)
                xb_free[par] = (sV, sV.n)
                if tap == "state" and not pre:
                    tapdump(COS)
                    return
                if tap == "mt" and not pre:
                    tapdump(VB)
                    return

                if pre:
                    if tap is not None and kind == "main":
                        pass
                    return

                t0 = i * Wt
                for c in range(8):
                    bank = PB3 if c % 2 == 0 else PEx

                    def ft(t, c=c, bank=bank):
                        bank.guard_write(t)
                        mm = t.matmul(out=bank.ps[:, 0:Wt], lhsT=lngrow[:, c * 128:(c + 1) * 128],
                                      rhs=iota_t[par][0:1, 0:Wt], start=True, stop=True)
                        sT.inc(mm)
                        bank.ready = (sT, sT.n)
                    K.T(ft)

                    def fa(a, c=c, bank=bank):
                        bank.wait_ready(a)
                        sS.inc(a.activation(out=GB(c), in_=bank.ps[:, 0:Wt],
                                            func=AF.Sigmoid, bias=LNB, scale=1.0))
                    K.A(fa)
                    bank.add_reader(sS)
                sse = sS.n

                def Vout(v):
                    v.wait_ge(sS.h, sse)
                    for c in range(8):
                        v.tensor_tensor(out=VB(c), in0=VB(c), in1=GB(c), op=ALU.mult)
                    sV.inc(v.last)
                    v.wait_ge(sV.h, sV.n)
                    for c in range(8):
                        v.tensor_tensor(out=OBF(c), in0=VB(c), in1=KBf(c, 1, Wt + 1), op=ALU.mult)
                    sV.inc(v.last)
                K.V(Vout)
                obf_ready = sV.n

                if tap == "mem" and not pre:
                    tapdump(VB)
                    return
                for oc in range(8):
                    wpar = oc % 2

                    wn_box = []

                    def Yw(sync, oc=oc, wpar=wpar):
                        wn_box.append(wload(sync, "o", wo, wpar,
                                            oT_s[oc].rearrange("kc p m -> p kc m"), 8, 128))
                    K.Y(Yw)
                    wn = wn_box[0]
                    bank = PP[oc % 2]

                    def ft(t, oc=oc, wpar=wpar, wn=wn, bank=bank):
                        bank.guard_write(t)
                        t.wait_ge(wn[0].h, wn[1])
                        if oc == 0:
                            t.wait_ge(sV.h, obf_ready)
                        mm = None
                        for kc in range(8):
                            mm = t.matmul(out=bank.ps[:, 0:Wt],
                                          lhsT=wo[wpar][:, kc * 128:(kc + 1) * 128],
                                          rhs=obf[:, kc * Wt:(kc + 1) * Wt],
                                          start=(kc == 0), stop=(kc == 7))
                        sT.inc(mm)
                        bank.ready = (sT, sT.n)
                    K.T(ft)
                    stream_free[("o", wpar)] = (sT, sT.n)

                    def fa(a, oc=oc, bank=bank):
                        bank.wait_ready(a)
                        sS.inc(a.activation(out=QK(oc), in_=bank.ps[:, 0:Wt], func=AF.Identity))
                    K.A(fa)
                    bank.add_reader(sS)
                y_ready = (sS, sS.n)

                if tap == "y" and not pre:
                    tapdump(QK)
                    return
                ln_stats(QK, Wt, y_ready)
                ln_rows(Wt, with_beta=False)
                bcast_rows(Wt, [(PB1, 0, 4), (PB2, 0, 7)])

                def Vxa(v):
                    PB1.wait_ready(v)
                    PB2.wait_ready(v)
                    for c in range(8):
                        apar = c % 2
                        v.tensor_tensor(out=att2[:, apar * Wt:(apar + 1) * Wt], in0=QK(c),
                                        in1=PB1.ps[:, 0:Wt], op=ALU.mult)
                        v.tensor_tensor(out=att2[:, apar * Wt:(apar + 1) * Wt],
                                        in0=att2[:, apar * Wt:(apar + 1) * Wt],
                                        in1=PB2.ps[:, 0:Wt], op=ALU.subtract)
                        v.tensor_scalar(out=att2[:, apar * Wt:(apar + 1) * Wt],
                                        in0=att2[:, apar * Wt:(apar + 1) * Wt],
                                        scalar1=onw(c), scalar2=onb(c), op0=ALU.mult, op1=ALU.add)
                        v.scalar_tensor_tensor(out=XN(c), in0=att2[:, apar * Wt:(apar + 1) * Wt],
                                               scalar=g1c(c), in1=XN(c), op0=ALU.mult, op1=ALU.add)
                    sV.inc(v.last)
                K.V(Vxa)
                PB1.add_reader(sV)
                PB2.add_reader(sV)
                xa_ready = (sV, sV.n)
                if tap == "xa":
                    tapdump(XN)
                    return

                ln_stats(XN, Wt, xa_ready)
                ln_rows(Wt, with_beta=False)
                bcast_rows(Wt, [(PB1, 0, 4), (PB2, 0, 7)])

                def Vh2(v):
                    PB1.wait_ready(v)
                    PB2.wait_ready(v)
                    for c in range(8):
                        apar = c % 2
                        v.tensor_tensor(out=att2[:, apar * Wt:(apar + 1) * Wt], in0=XN(c),
                                        in1=PB1.ps[:, 0:Wt], op=ALU.mult)
                        v.tensor_tensor(out=att2[:, apar * Wt:(apar + 1) * Wt],
                                        in0=att2[:, apar * Wt:(apar + 1) * Wt],
                                        in1=PB2.ps[:, 0:Wt], op=ALU.subtract)
                        v.tensor_scalar(out=OBF(c), in0=att2[:, apar * Wt:(apar + 1) * Wt],
                                        scalar1=ln2w(c), scalar2=ln2b(c), op0=ALU.mult, op1=ALU.add)
                    sV.inc(v.last)
                K.V(Vh2)
                PB1.add_reader(sV)
                PB2.add_reader(sV)
                h2_ready = sV.n

                sgu_free = [None, None]
                for fc in range(NFC):
                    wpar = fc % 2

                    wn_box = []

                    def Yw(sync, fc=fc, wpar=wpar):
                        wn_box.append(wload(sync, "gate", wgate, wpar,
                                            gateT_s[fc].rearrange("kc p m -> p kc m"), 8, 128))
                        wn_box.append(wload(sync, "up", wup, wpar,
                                            upT_s[fc].rearrange("kc p m -> p kc m"), 8, 128))
                    K.Y(Yw)
                    wng, wnu = wn_box
                    bgate = PP[0]
                    bup = PP[1]

                    def ft(t, fc=fc, wpar=wpar, wng=wng, wnu=wnu):
                        bgate.guard_write(t)
                        t.wait_ge(wng[0].h, wng[1])
                        t.wait_ge(wnu[0].h, wnu[1])
                        if fc == 0:
                            t.wait_ge(sV.h, h2_ready)
                        mm = None
                        for kc in range(8):
                            mm = t.matmul(out=bgate.ps[:, 0:Wt],
                                          lhsT=wgate[wpar][:, kc * 128:(kc + 1) * 128],
                                          rhs=obf[:, kc * Wt:(kc + 1) * Wt],
                                          start=(kc == 0), stop=(kc == 7))
                        sT.inc(mm)
                        bgate.ready = (sT, sT.n)
                        bup.guard_write(t)
                        mm = None
                        for kc in range(8):
                            mm = t.matmul(out=bup.ps[:, 0:Wt],
                                          lhsT=wup[wpar][:, kc * 128:(kc + 1) * 128],
                                          rhs=obf[:, kc * Wt:(kc + 1) * Wt],
                                          start=(kc == 0), stop=(kc == 7))
                        sT.inc(mm)
                        bup.ready = (sT, sT.n)
                    K.T(ft)
                    stream_free[("gate", wpar)] = (sT, sT.n)
                    stream_free[("up", wpar)] = (sT, sT.n)

                    sgf = sgu_free[fc % 2]

                    def fa(a, fc=fc, sgf=sgf):
                        bgate.wait_ready(a)
                        if sgf is not None:
                            a.wait_ge(sgf[0].h, sgf[1])
                        sS.inc(a.activation(out=sg_t[:, (fc % 2) * Wt:(fc % 2 + 1) * Wt],
                                            in_=bgate.ps[:, 0:Wt], func=AF.Silu))
                    K.A(fa)
                    bgate.add_reader(sS)
                    ssn = sS.n

                    def fv(v, fc=fc, ssn=ssn):
                        bup.wait_ready(v)
                        v.wait_ge(sS.h, ssn)
                        sV.inc(v.tensor_tensor(out=mbf[:, fc * Wt:(fc + 1) * Wt],
                                               in0=bup.ps[:, 0:Wt],
                                               in1=sg_t[:, (fc % 2) * Wt:(fc % 2 + 1) * Wt],
                                               op=ALU.mult))
                    K.V(fv)
                    bup.add_reader(sV)
                    sgu_free[fc % 2] = (sV, sV.n)
                m_ready = sV.n

                for oc in range(8):
                    wpar = oc % 2

                    wn_box = []

                    def Yw(sync, oc=oc, wpar=wpar):
                        wn_box.append(wload(sync, "down", wdown, wpar,
                                            downT_s[oc].rearrange("fc p m -> p fc m"), 32, 128))
                    K.Y(Yw)
                    wn = wn_box[0]
                    bank = PP[oc % 2]

                    def ft(t, oc=oc, wpar=wpar, wn=wn, bank=bank):
                        bank.guard_write(t)
                        t.wait_ge(wn[0].h, wn[1])
                        if oc == 0:
                            t.wait_ge(sV.h, m_ready)
                        mm = None
                        for fc in range(NFC):
                            mm = t.matmul(out=bank.ps[:, 0:Wt],
                                          lhsT=wdown[wpar][:, fc * 128:(fc + 1) * 128],
                                          rhs=mbf[:, fc * Wt:(fc + 1) * Wt],
                                          start=(fc == 0), stop=(fc == NFC - 1))
                        sT.inc(mm)
                        bank.ready = (sT, sT.n)
                    K.T(ft)
                    stream_free[("down", wpar)] = (sT, sT.n)

                    def fv(v, oc=oc, bank=bank):
                        bank.wait_ready(v)
                        apar = oc % 2
                        v.scalar_tensor_tensor(out=att2[:, apar * Wt:(apar + 1) * Wt],
                                               in0=bank.ps[:, 0:Wt], scalar=g2c(oc),
                                               in1=XN(oc), op0=ALU.mult, op1=ALU.add)
                        sV.inc(v.last)
                        v.wait_ge(sV.h, sV.n)
                        sV.inc(v.tensor_tensor(out=X(oc), in0=att2[:, apar * Wt:(apar + 1) * Wt],
                                               in1=X(oc), op=ALU.add))
                    K.V(fv)
                    bank.add_reader(sV)
                s1_ready = sV.n

                oqp = oq0 if par == 0 else oq1

                def As1(a):
                    a.wait_ge(sV.h, s1_ready)
                    dst = out_t.rearrange("(c p) t -> c p t", p=128)
                    for c in range(8):
                        oqp.inc(a.dma_start(out=dst[c][:, i * Wt:(i + 1) * Wt],
                                            in_=X(c)), 16)
                K.A(As1)
                xb_free[par] = (oqp, oqp.n)

            for ti in range(len(tile_list) if max_tiles is None else min(max_tiles, len(tile_list))):
                emit_tile(ti)
            if cast_b_pos[0] < len(cast_b_blocks):
                emit_cast(cast_b_blocks[cast_b_pos[0]:])
            cast_b_done = (aq[0].n, aq[1].n)
            p1_store_done = (oq0.n, oq1.n)
            p1_v, p1_s = sV.n, sS.n
        midctx.close()

        if phase2:
            with contextlib.ExitStack() as p2ctx:
                E2 = p2ctx.enter_context
                Wt = W2T
                xnb2 = E2(nc.sbuf_tensor("xnb2", [128, 8 * Wt], F32))
                xnbf = E2(nc.sbuf_tensor("xnbf", [128, 8 * Wt], BF16))
                tgt2 = E2(nc.sbuf_tensor("tgt2", [128, 8 * Wt], F32))
                hbf = E2(nc.sbuf_tensor("hbf", [128, 32 * Wt], BF16))
                w2sb = E2(nc.sbuf_tensor("w2sb", [128, 32 * 1024], BF16))
                w1sb = [E2(nc.sbuf_tensor(f"w1sb{p}", [128, 1024], BF16)) for p in range(2)]
                wbc = E2(nc.sbuf_tensor("wbc", [128, Wt], F32))
                hrel = E2(nc.sbuf_tensor("hrel", [128, 2 * Wt], F32))
                sq2 = E2(nc.sbuf_tensor("sq2", [128, 2 * Wt], F32))
                rows2 = E2(nc.sbuf_tensor("rows2", [1, 8 * Wt], F32))
                S1b = E2(nc.sbuf_tensor("S1b", [128, 8 * Wt], F32))
                lgT = E2(nc.sbuf_tensor("lgT", [128, 4 * 8], F32))
                top8 = E2(nc.sbuf_tensor("top8", [128, 4 * 8], F32))
                msk = E2(nc.sbuf_tensor("msk", [128, 3 * 8], F32))
                pcol = E2(nc.sbuf_tensor("pcol", [128, 4], F32))
                wtok = E2(nc.sbuf_tensor("wtok", [128, 4 * 8], F32))
                wT_sb = E2(nc.sbuf_tensor("wT_sb", [8, Wt], F32))
                b2sb = E2(nc.sbuf_tensor("b2sb", [8, D], F32))
                b1sb = E2(nc.sbuf_tensor("b1sb", [128, 256], F32))
                eselsb = E2(nc.sbuf_tensor("eselsb", [8, D], F32))
                identsb = E2(nc.sbuf_tensor("identsb", [128, 128], F32))
                wrf = E2(nc.sbuf_tensor("wrf", [128, 64], F32))
                wrbf = E2(nc.sbuf_tensor("wrbf", [128, 64], BF16))
                brb = E2(nc.sbuf_tensor("brb", [128, 8], F32))
                hp = [Bank(E2(nc.psum_tensor(f"hp{i}", [128, 512], F32))) for i in range(2)]
                tp = [Bank(E2(nc.psum_tensor(f"tp{i}", [128, 512], F32))) for i in range(2)]
                pr = Bank(E2(nc.psum_tensor("pr", [128, 512], F32)))
                pst = Bank(E2(nc.psum_tensor("pst", [128, 512], F32)))
                pb1 = Bank(E2(nc.psum_tensor("pb1", [128, 512], F32)))
                pb2 = Bank(E2(nc.psum_tensor("pb2", [128, 512], F32)))

                def R2(idx, width=Wt):
                    return rows2[0:1, idx * Wt:idx * Wt + width]

                def Y2c(sync):
                    sync.wait_ge(oq0.h, p1_store_done[0])
                    sync.wait_ge(oq1.h, p1_store_done[1])
                    sync.wait_ge(aq[0].h, cast_b_done[0])
                    sync.wait_ge(aq[1].h, cast_b_done[1])
                    sync.wait_ge(sV.h, p1_v)
                    sync.wait_ge(sS.h, p1_s)
                    for dst, src in [(b2sb[:], b2_in), (b1sb[:], b1_in),
                                     (eselsb[:], esel_in), (identsb[:], ident_in),
                                     (wrf[:].rearrange("p (k m) -> p k m", k=8), wrT_in.rearrange("kc p m -> p kc m"))]:
                        wq.inc(sync.dma_start(out=dst, in_=src), 16)
                K.Y(Y2c)
                n2c = wq.n



                def Y2b(sync):
                    wq.inc(sync.dma_start(out=wT_sb[0:1, 0:8], in_=br_in), 16)
                K.Y(Y2b)
                n2b = wq.n

                def T2b(t):
                    t.wait_ge(sV.h, p1_v)
                    t.wait_ge(sS.h, p1_s)
                    pr.guard_write(t)
                    t.wait_ge(wq.h, n2b)
                    sT.inc(t.matmul(out=pr.ps[:, 0:8], lhsT=ones_row[:], rhs=wT_sb[0:1, 0:8],
                                    start=True, stop=True))
                    pr.ready = (sT, sT.n)
                K.T(T2b)

                def V2b(v):
                    pr.wait_ready(v)
                    sV.inc(v.tensor_copy(out=brb[:], in_=pr.ps[:, 0:8]))
                K.V(V2b)
                pr.add_reader(sV)

                sq2_free = [None, None]

                def ln2_stats(src, wait_src):
                    def ft0(t):
                        pst.guard_write(t)
                        if wait_src is not None:
                            t.wait_ge(wait_src[0].h, wait_src[1])
                        mm = None
                        for c in range(8):
                            mm = t.matmul(out=pst.ps[0:1, 0:Wt], lhsT=ones_col[:], rhs=src(c),
                                          start=(c == 0), stop=(c == 7))
                        sT.inc(mm)
                    K.T(ft0)
                    for c in range(8):
                        spar = c % 2
                        fr = sq2_free[spar]

                        def fa(a, c=c, spar=spar, fr=fr):
                            if c == 0 and wait_src is not None:
                                a.wait_ge(wait_src[0].h, wait_src[1])
                            if fr is not None:
                                a.wait_ge(fr[0].h, fr[1])
                            sS.inc(a.activation(out=sq2[:, spar * Wt:(spar + 1) * Wt],
                                                in_=src(c), func=AF.Square))
                        K.A(fa)
                        ssn = sS.n

                        def ft(t, c=c, spar=spar, ssn=ssn):
                            t.wait_ge(sS.h, ssn)
                            mm = t.matmul(out=pst.ps[32:33, 0:Wt], lhsT=ones_col[:],
                                          rhs=sq2[:, spar * Wt:(spar + 1) * Wt],
                                          start=(c == 0), stop=(c == 7))
                            sT.inc(mm)
                        K.T(ft)
                        sq2_free[spar] = (sT, sT.n)
                    pst.ready = (sT, sT.n)

                def ln2_rows():
                    def fv(v):
                        pst.wait_ready(v)
                        v.tensor_scalar(out=R2(0), in0=pst.ps[0:1, 0:Wt],
                                        scalar1=1.0 / 1024.0, scalar2=None, op0=ALU.mult)
                        v.tensor_scalar(out=R2(1), in0=pst.ps[32:33, 0:Wt],
                                        scalar1=1.0 / 1024.0, scalar2=None, op0=ALU.mult)
                        sV.inc(v.last)
                        v.wait_ge(sV.h, sV.n)
                        v.tensor_tensor(out=R2(2), in0=R2(0), in1=R2(0), op=ALU.mult)
                        sV.inc(v.last)
                        v.wait_ge(sV.h, sV.n)
                        sV.inc(v.tensor_tensor(out=R2(2), in0=R2(1), in1=R2(2), op=ALU.subtract))
                    K.V(fv)
                    pst.add_reader(sV)
                    sv0 = sV.n

                    def fa(a):
                        a.wait_ge(sV.h, sv0)
                        sS.inc(a.activation(out=R2(3), in_=R2(2), func=AF.Ln, bias=1e-5))
                        a.wait_ge(sS.h, sS.n)
                        sS.inc(a.activation(out=R2(4), in_=R2(3), func=AF.Exp, scale=-0.5))
                    K.A(fa)
                    ss0 = sS.n

                    def fv2(v):
                        v.wait_ge(sS.h, ss0)
                        sV.inc(v.tensor_tensor(out=R2(7), in0=R2(0), in1=R2(4), op=ALU.mult))
                    K.V(fv2)

                def bcast2():
                    rr = (sV, sV.n)
                    for bk, ri in ((pb1, 4), (pb2, 7)):
                        def ft(t, bk=bk, ri=ri):
                            bk.guard_write(t)
                            t.wait_ge(rr[0].h, rr[1])
                            sT.inc(t.matmul(out=bk.ps[:, 0:Wt], lhsT=ones_row[:], rhs=R2(ri),
                                            start=True, stop=True))
                            bk.ready = (sT, sT.n)
                        K.T(ft)

                XN2 = lambda c: xnb2[:, c * Wt:(c + 1) * Wt]
                TG = lambda c: tgt2[:, c * Wt:(c + 1) * Wt]
                S1c = lambda c: S1b[:, c * Wt:(c + 1) * Wt]

                p2_prev_done = None
                for t2 in range(NT2):
                    t0 = t2 * Wt

                    def tapdump2(bufsl, t0=t0):
                        def As(a):
                            a.wait_ge(sV.h, sV.n)
                            a.wait_ge(sS.h, sS.n)
                            dst = out_t.rearrange("(c p) t -> c p t", p=128)
                            for c in range(8):
                                oq2.inc(a.dma_start(out=dst[c][:, t0:t0 + Wt],
                                                    in_=bufsl(c)), 16)
                        K.A(As)
                    def Y2x(sync, t0=t0, t2=t2):
                        if t2 == 0:
                            sync.wait_ge(oq0.h, p1_store_done[0])
                            sync.wait_ge(oq1.h, p1_store_done[1])
                            sync.wait_ge(aq[0].h, cast_b_done[0])
                            sync.wait_ge(aq[1].h, cast_b_done[1])
                        if p2_prev_done is not None:
                            sync.wait_ge(p2_prev_done[0].h, p2_prev_done[1])
                        src_x = x_main.rearrange("(c p) t -> c p t", p=128)
                        src_s = out_t.rearrange("(c p) t -> c p t", p=128)
                        for c in range(8):
                            x2q.inc(sync.dma_start(out=XN2(c),
                                                   in_=src_x[c][:, t0:t0 + Wt]), 16)
                            x2q.inc(sync.dma_start(out=S1c(c),
                                                   in_=src_s[c][:, t0:t0 + Wt]), 16)
                    K.Y(Y2x)
                    x2rdy = (x2q, x2q.n)

                    ln2_stats(XN2, x2rdy)
                    ln2_rows()
                    bcast2()

                    def V2xn(v):
                        pb1.wait_ready(v)
                        pb2.wait_ready(v)
                        for c in range(8):
                            v.tensor_tensor(out=XN2(c), in0=XN2(c), in1=pb1.ps[:, 0:Wt], op=ALU.mult)
                        sV.inc(v.last)
                        v.wait_ge(sV.h, sV.n)
                        for c in range(8):
                            v.tensor_tensor(out=XN2(c), in0=XN2(c), in1=pb2.ps[:, 0:Wt], op=ALU.subtract)
                        sV.inc(v.last)
                        v.wait_ge(sV.h, sV.n)
                        for c in range(8):
                            v.tensor_copy(out=xnbf[:, c * Wt:(c + 1) * Wt], in_=XN2(c))
                        sV.inc(v.last)
                    K.V(V2xn)
                    pb1.add_reader(sV)
                    pb2.add_reader(sV)
                    xn_ready = sV.n

                    for s in range(4):
                        def ft(t, s=s):
                            pr.guard_write(t)
                            if s == 0:
                                t.wait_ge(sV.h, xn_ready)
                            mm = None
                            for kc in range(8):
                                mm = t.matmul(out=pr.ps[:, 0:8],
                                              lhsT=xnb2[:, kc * Wt + s * 128: kc * Wt + (s + 1) * 128],
                                              rhs=wrf[:, kc * 8:(kc + 1) * 8],
                                              start=(kc == 0), stop=(kc == 7))
                            sT.inc(mm)
                            pr.ready = (sT, sT.n)
                        K.T(ft)

                        def fv(v, s=s):
                            pr.wait_ready(v)
                            v.wait_ge(sV.h, sV.n)
                            sV.inc(v.tensor_tensor(out=lgT[:, s * 8:(s + 1) * 8],
                                                   in0=pr.ps[:, 0:8], in1=brb[:], op=ALU.add))
                            v.wait_ge(sV.h, sV.n)
                            sV.inc(v.max(out=top8[:, s * 8:(s + 1) * 8], in_=lgT[:, s * 8:(s + 1) * 8]))
                            v.wait_ge(sV.h, sV.n)
                            sV.inc(v.tensor_tensor(out=pcol[:, s:s + 1], in0=top8[:, s * 8:s * 8 + 1],
                                                   in1=top8[:, s * 8 + 1:s * 8 + 2], op=ALU.subtract))
                        K.V(fv)
                        pr.add_reader(sV)
                        svp = sV.n

                        def fa(a, s=s, svp=svp):
                            a.wait_ge(sV.h, svp)
                            sS.inc(a.activation(out=pcol[:, s:s + 1], in_=pcol[:, s:s + 1],
                                                func=AF.Sigmoid))
                        K.A(fa)
                        ssp = sS.n

                        def fv2(v, s=s, ssp=ssp):
                            v.wait_ge(sS.h, ssp)
                            v.tensor_scalar(out=msk[:, 0:8], in0=lgT[:, s * 8:(s + 1) * 8],
                                            scalar1=top8[:, s * 8:s * 8 + 1], scalar2=None, op0=ALU.is_ge)
                            v.tensor_scalar(out=msk[:, 8:16], in0=lgT[:, s * 8:(s + 1) * 8],
                                            scalar1=top8[:, s * 8 + 1:s * 8 + 2], scalar2=None, op0=ALU.is_ge)
                            sV.inc(v.last)
                            v.wait_ge(sV.h, sV.n)
                            v.tensor_tensor(out=msk[:, 8:16], in0=msk[:, 8:16], in1=msk[:, 0:8], op=ALU.subtract)
                            sV.inc(v.last)
                            v.wait_ge(sV.h, sV.n)
                            v.tensor_scalar(out=msk[:, 16:24], in0=msk[:, 8:16],
                                            scalar1=pcol[:, s:s + 1], scalar2=None, op0=ALU.mult)
                            sV.inc(v.last)
                            v.wait_ge(sV.h, sV.n)
                            v.tensor_tensor(out=msk[:, 8:16], in0=msk[:, 8:16], in1=msk[:, 16:24], op=ALU.subtract)
                            sV.inc(v.last)
                            v.wait_ge(sV.h, sV.n)
                            v.scalar_tensor_tensor(out=wtok[:, s * 8:(s + 1) * 8], in0=msk[:, 0:8],
                                                   scalar=pcol[:, s:s + 1], in1=msk[:, 8:16],
                                                   op0=ALU.mult, op1=ALU.add)
                            sV.inc(v.last)
                            v.wait_ge(sV.h, sV.n)
                        K.V(fv2)
                        svw = sV.n

                        def ft2(t, s=s, svw=svw):
                            pr.guard_write(t)
                            t.wait_ge(sV.h, svw)
                            sT.inc(t.transpose(out=pr.ps[0:8, 0:128], in_=wtok[:, s * 8:(s + 1) * 8],
                                               identity=identsb[:]))
                            pr.ready = (sT, sT.n)
                        K.T(ft2)

                        def fa2(a, s=s):
                            pr.wait_ready(a)
                            sS.inc(a.activation(out=wT_sb[0:8, s * 128:(s + 1) * 128],
                                                in_=pr.ps[0:8, 0:128], func=AF.Identity))
                        K.A(fa2)
                        pr.add_reader(sS)
                    wT_ready = sS.n

                    for oc in range(8):
                        bank = tp[oc % 2]

                        def ft(t, oc=oc, bank=bank):
                            bank.guard_write(t)
                            if oc == 0:
                                t.wait_ge(sS.h, wT_ready)
                            sT.inc(t.matmul(out=bank.ps[:, 0:Wt], lhsT=b2sb[0:8, oc * 128:(oc + 1) * 128],
                                            rhs=wT_sb[0:8, 0:Wt], start=True, stop=True))
                            bank.ready = (sT, sT.n)
                        K.T(ft)

                        def fa(a, oc=oc, bank=bank):
                            bank.wait_ready(a)
                            sS.inc(a.activation(out=TG(oc), in_=bank.ps[:, 0:Wt], func=AF.Identity))
                        K.A(fa)
                        bank.add_reader(sS)

                    if tap == "xn2":
                        tapdump2(XN2)
                        p2_prev_done = (oq2, oq2.n)
                        continue
                    if tap == "s1rb":
                        tapdump2(S1c)
                        p2_prev_done = (oq2, oq2.n)
                        continue
                    if tap == "wrow":
                        pass
                    w2_loaded = None
                    w2_user = None
                    for e in range(NE):
                        def Yw2(sync, e=e, w2_user=w2_user):
                            if w2_user is not None:
                                sync.wait_ge(w2_user[0].h, w2_user[1])
                            if e == 0:
                                sync.wait_ge(aq[0].h, cast_b_done[0])
                                sync.wait_ge(aq[1].h, cast_b_done[1])
                            for q in range(4):
                                src = w2T_s[e, q * 8:(q + 1) * 8].rearrange("fc p m -> p fc m")
                                dst = w2sb[:, q * 8 * 1024:(q + 1) * 8 * 1024].rearrange(
                                    "p (k m) -> p k m", k=8)
                                w2q.inc(sync.dma_start(out=dst, in_=src), 16)
                        K.Y(Yw2)
                        w2_loaded = w2q.n

                        def ft(t, e=e):
                            pr.guard_write(t)
                            t.wait_ge(sS.h, wT_ready)
                            sT.inc(t.matmul(out=pr.ps[:, 0:Wt], lhsT=eselsb[0:8, e * 128:(e + 1) * 128],
                                            rhs=wT_sb[0:8, 0:Wt], start=True, stop=True))
                            pr.ready = (sT, sT.n)
                        K.T(ft)

                        def fa(a, e=e):
                            pr.wait_ready(a)
                            a.wait_ge(sV.h, sV.n)
                            sS.inc(a.activation(out=wbc[:], in_=pr.ps[:, 0:Wt], func=AF.Identity))
                        K.A(fa)
                        pr.add_reader(sS)
                        wbc_ready = sS.n

                        hrel_free = [None, None] if e == 0 else hrel_free
                        for fc in range(NFC):
                            wpar = fc % 2

                            def Yw1(sync, e=e, fc=fc, wpar=wpar):
                                fr = stream_free.get(("w1", wpar))
                                if fr is not None:
                                    sync.wait_ge(fr[0].h, fr[1])
                                if e == 0 and fc < 2:
                                    sync.wait_ge(aq[0].h, cast_b_done[0])
                                    sync.wait_ge(aq[1].h, cast_b_done[1])
                                STRM["w1"][wpar].inc(sync.dma_start(
                                    out=w1sb[wpar][:].rearrange("p (k m) -> p k m", k=8),
                                    in_=w1T_s[e, fc].rearrange("kc p m -> p kc m")), 16)
                            K.Y(Yw1)
                            wn = (STRM["w1"][wpar], STRM["w1"][wpar].n)
                            bank = hp[fc % 2]

                            def ft1(t, fc=fc, wpar=wpar, wn=wn, bank=bank):
                                bank.guard_write(t)
                                t.wait_ge(wn[0].h, wn[1])
                                if fc == 0:
                                    t.wait_ge(sV.h, xn_ready)
                                mm = None
                                for kc in range(8):
                                    mm = t.matmul(out=bank.ps[:, 0:Wt],
                                                  lhsT=w1sb[wpar][:, kc * 128:(kc + 1) * 128],
                                                  rhs=xnbf[:, kc * Wt:(kc + 1) * Wt],
                                                  start=(kc == 0), stop=(kc == 7))
                                sT.inc(mm)
                                bank.ready = (sT, sT.n)
                            K.T(ft1)
                            stream_free[("w1", wpar)] = (sT, sT.n)

                            hrf = hrel_free[fc % 2]

                            def fa1(a, e=e, fc=fc, bank=bank, hrf=hrf):
                                bank.wait_ready(a)
                                if fc == 0:
                                    a.wait_ge(sS.h, wbc_ready)
                                if hrf is not None:
                                    a.wait_ge(hrf[0].h, hrf[1])
                                sS.inc(a.activation(out=hrel[:, (fc % 2) * Wt:(fc % 2 + 1) * Wt],
                                                    in_=bank.ps[:, 0:Wt], func=AF.Relu,
                                                    bias=b1sb[:, e * 32 + fc:e * 32 + fc + 1], scale=1.0))
                            K.A(fa1)
                            bank.add_reader(sS)
                            ssn = sS.n

                            def fv1(v, fc=fc, ssn=ssn):
                                v.wait_ge(sS.h, ssn)
                                sV.inc(v.tensor_tensor(out=hbf[:, fc * Wt:(fc + 1) * Wt],
                                                       in0=hrel[:, (fc % 2) * Wt:(fc % 2 + 1) * Wt],
                                                       in1=wbc[:], op=ALU.mult))
                            K.V(fv1)
                            hrel_free[fc % 2] = (sV, sV.n)
                        h_ready = sV.n

                        for oc in range(8):
                            bank = tp[oc % 2]

                            def ft2(t, oc=oc, bank=bank, w2_loaded=w2_loaded):
                                bank.guard_write(t)
                                t.wait_ge(w2q.h, w2_loaded)
                                if oc == 0:
                                    t.wait_ge(sV.h, h_ready)
                                mm = None
                                for fc in range(NFC):
                                    mm = t.matmul(out=bank.ps[:, 0:Wt],
                                                  lhsT=w2sb[:, fc * 1024 + oc * 128: fc * 1024 + (oc + 1) * 128],
                                                  rhs=hbf[:, fc * Wt:(fc + 1) * Wt],
                                                  start=(fc == 0), stop=(fc == NFC - 1))
                                sT.inc(mm)
                                bank.ready = (sT, sT.n)
                            K.T(ft2)

                            def fv2(v, oc=oc, bank=bank):
                                bank.wait_ready(v)
                                sV.inc(v.tensor_tensor(out=TG(oc), in0=TG(oc),
                                                       in1=bank.ps[:, 0:Wt], op=ALU.add))
                            K.V(fv2)
                            bank.add_reader(sV)
                        w2_user = (sT, sT.n)

                    if tap == "tgt2":
                        tapdump2(TG)
                        p2_prev_done = (oq2, oq2.n)
                        continue
                    def V2z(v):
                        v.wait_ge(sV.h, sV.n)
                        for c in range(8):
                            v.tensor_tensor(out=TG(c), in0=TG(c), in1=XN2(c), op=ALU.add)
                        sV.inc(v.last)
                        v.wait_ge(sV.h, sV.n)
                    K.V(V2z)
                    z_ready = (sV, sV.n)
                    ln2_stats(TG, z_ready)
                    ln2_rows()
                    bcast2()

                    def V2f(v):
                        pb1.wait_ready(v)
                        pb2.wait_ready(v)
                        for c in range(8):
                            v.tensor_tensor(out=TG(c), in0=TG(c), in1=pb1.ps[:, 0:Wt], op=ALU.mult)
                        sV.inc(v.last)
                        v.wait_ge(sV.h, sV.n)
                        for c in range(8):
                            v.tensor_tensor(out=TG(c), in0=TG(c), in1=pb2.ps[:, 0:Wt], op=ALU.subtract)
                        sV.inc(v.last)
                        v.wait_ge(sV.h, sV.n)
                        for c in range(8):
                            v.tensor_scalar(out=TG(c), in0=TG(c), scalar1=fgamc(c), scalar2=fbetac(c),
                                            op0=ALU.mult, op1=ALU.add)
                        sV.inc(v.last)
                        v.wait_ge(sV.h, sV.n)
                        for c in range(8):
                            v.tensor_tensor(out=S1c(c), in0=S1c(c), in1=TG(c), op=ALU.add)
                        sV.inc(v.last)
                    K.V(V2f)
                    pb1.add_reader(sV)
                    pb2.add_reader(sV)
                    f_ready = sV.n
                    if tap == "F":
                        tapdump2(TG)
                        p2_prev_done = (oq2, oq2.n)
                        continue

                    def A2o(a, t0=t0):
                        a.wait_ge(sV.h, f_ready)
                        dst = out_t.rearrange("(c p) t -> c p t", p=128)
                        for c in range(8):
                            oq2.inc(a.dma_start(out=dst[c][:, t0:t0 + Wt],
                                                in_=S1c(c)), 16)
                    K.A(A2o)
                    p2_prev_done = (oq2, oq2.n)

        def Yfin(sync):
            for s in [oq0, oq1, oq2, wq, x2q, w2q] + aq + cq + xq + [
                    sm for pair in STRM.values() for sm in pair]:
                sync.wait_ge(s.h, s.n)
        K.Y(Yfin)

    return nc


def emit_tap(K, buf, Wt, i, out_t, n_main, oq, ready):
    def As(a):
        a.wait_ge(ready[0].h, ready[1])
        dst = out_t.rearrange("(c p) t -> c p t", p=128)
        for c in range(8):
            oq.inc(a.dma_start(out=dst[c][:, i * Wt:(i + 1) * Wt],
                               in_=buf[:, c * Wt:(c + 1) * Wt]), 16)
    K.A(As)



def _fm(a):
    return np.ascontiguousarray(np.asarray(a, dtype=np.float32).T)


def _vec8(a):
    return np.ascontiguousarray(np.asarray(a, dtype=np.float32).reshape(8, 128).T)


_FREQ_BITS = np.array([
    1065353216, 1065665483, 1065989374, 1066325321, 1066673775, 1067035200,
    1067410078, 1067798912, 1068202220, 1068620541, 1069054434, 1069504479,
    1069971278, 1070455452, 1070957650, 1071478542, 1072018825, 1072579220,
    1073160475, 1073752596, 1074065263, 1074389570, 1074725949, 1075074850,
    1075436740, 1075812100, 1076201433, 1076605259, 1077024117, 1077458567,
    1077909190, 1078376587, 1078861384, 1079364226, 1079885788, 1080426764,
    1080987878, 1081569880, 1082151989, 1082465059, 1082789782, 1083126593,
    1083475942, 1083838295, 1084214137, 1084603970, 1085008315, 1085427711,
    1085862721, 1086313922, 1086781920, 1087267338, 1087770827, 1088293058,
    1088834729, 1089396564, 1089979313, 1090551397, 1090864869, 1091190009,
    1091527252, 1091877050, 1092239869, 1092616192], dtype=np.uint32)


def _cos_table():
    base = _FREQ_BITS.view(np.float32)
    fr = (base[None, :] * np.arange(H, dtype=np.float32)[:, None]).reshape(-1)
    t_ = np.arange(8192, dtype=np.float32)
    ang = t_[:, None] * fr[None, :]
    ctab = np.cos(ang.astype(np.float64)).astype(np.float32)
    return np.ascontiguousarray(ctab.T)


def _prep_weights(inp):
    w = {}
    def t8(a):
        aT = np.ascontiguousarray(np.asarray(a, np.float32).T)
        return np.ascontiguousarray(aT.reshape(8, 128, 8, 128).transpose(2, 0, 1, 3))
    w["qkT"] = t8(inp["L0_attn_qk_w"])
    w["vT"] = t8(inp["L0_attn_v_w"])
    w["gT"] = t8(inp["L0_attn_g_w"])
    w["oT"] = t8(inp["L0_attn_o_w"])
    def tgu(a):
        aT = np.ascontiguousarray(np.asarray(a, np.float32).T)
        return np.ascontiguousarray(aT.reshape(8, 128, 32, 128).transpose(2, 0, 1, 3))
    w["gateT"] = tgu(inp["L0_attn_gate_w"])
    w["upT"] = tgu(inp["L0_attn_up_w"])
    dT = np.ascontiguousarray(np.asarray(inp["L0_attn_down_w"], np.float32).T)
    w["downT"] = np.ascontiguousarray(dT.reshape(32, 128, 8, 128).transpose(2, 0, 1, 3))
    w1 = np.asarray(inp["L0_ffn_W1"], np.float32)
    w1T = np.ascontiguousarray(np.transpose(w1, (0, 2, 1)))
    w["w1T"] = np.ascontiguousarray(
        w1T.reshape(8, 8, 128, 32, 128).transpose(0, 3, 1, 2, 4))
    w2 = np.asarray(inp["L0_ffn_W2"], np.float32)
    w2T = np.ascontiguousarray(np.transpose(w2, (0, 2, 1)))
    w["w2T"] = np.ascontiguousarray(w2T.reshape(8, 32, 128, 1024))
    w["wrT"] = np.ascontiguousarray(
        np.asarray(inp["L0_ffn_Wr"], np.float32).T.reshape(8, 128, 8))
    return w


def make_in_maps(inp, n_main=4096, n_pre=4096):
    x = np.asarray(inp["L0_attn_x"], np.float32)
    ctab = _cos_table()
    W = _prep_weights(inp)

    vec_names = [("ln1w", "L0_attn_ln1_w"), ("ln1b", "L0_attn_ln1_b"),
                 ("convb", "L0_attn_conv_b"), ("gb", "L0_attn_g_b"),
                 ("onw", "L0_attn_onorm_w"), ("onb", "L0_attn_onorm_b"),
                 ("ln2w", "L0_attn_ln2_w"), ("ln2b", "L0_attn_ln2_b"),
                 ("g1", "L0_attn_gamma1"), ("g2", "L0_attn_gamma2"),
                 ("fgam", "L0_ffn_gamma"), ("fbeta", "L0_ffn_beta")]
    vecs = np.concatenate([_vec8(inp[k]) for _, k in vec_names], axis=1)
    convw = np.ascontiguousarray(
        np.asarray(inp["L0_attn_conv_w"], np.float32).reshape(8, 128, 4)
        .transpose(1, 0, 2).reshape(128, 32))
    g16 = np.asarray(inp["L0_attn_gamma_attn"], np.float32).reshape(16)
    grep = np.repeat(g16, HD)
    gcol = _vec8(grep)
    grow = np.ascontiguousarray(grep[None, :])
    hsel = np.zeros((128, 128), np.float32)
    hselT = np.zeros((16, 1024), np.float32)
    for c in range(8):
        for p in range(128):
            h = 2 * c + p // 64
            hsel[p, c * 16 + h] = 1.0
            hselT[h, c * 128 + p] = 1.0
    ident = np.eye(128, dtype=np.float32)
    esel = np.zeros((8, 1024), np.float32)
    for e in range(8):
        esel[e, e * 128:(e + 1) * 128] = 1.0
    b2 = np.ascontiguousarray(np.asarray(inp["L0_ffn_b2"], np.float32))
    b1 = np.ascontiguousarray(
        np.asarray(inp["L0_ffn_b1"], np.float32).reshape(8, 32, 128)
        .transpose(2, 0, 1).reshape(128, 256))
    br = np.ascontiguousarray(np.asarray(inp["L0_ffn_br"], np.float32)[None, :])

    common = dict(vecs=vecs, convw=convw, gcol=gcol, grow=grow, hsel=hsel,
                  hselT=hselT, ident=ident, esel=esel, b2=b2, b1=b1, br=br,
                  wrT=W["wrT"], qkT=W["qkT"], vT=W["vT"], gT=W["gT"], oT=W["oT"],
                  gateT=W["gateT"], upT=W["upT"], downT=W["downT"],
                  w1T=W["w1T"], w2T=W["w2T"])

    zeros_x = np.zeros((D, n_pre), np.float32)
    iota_all = np.arange(8192, dtype=np.float32)[None, :]
    in_maps = []
    for core in range(8):
        b, j = core // 2, core % 2
        m = dict(common)
        m["x_main"] = _fm(x[b, j * n_main:(j + 1) * n_main, :])
        m["cos_main"] = np.ascontiguousarray(ctab[:, j * n_main:(j + 1) * n_main])
        m["iota_main"] = np.ascontiguousarray(iota_all[:, j * n_main:(j + 1) * n_main])
        if j == 1:
            m["x_pre"] = _fm(x[b, 0:n_pre, :])
            m["cos_pre"] = np.ascontiguousarray(ctab[:, 0:n_pre])
            m["cmask"] = np.ones((128, 1), np.float32)
        else:
            m["x_pre"] = zeros_x
            m["cos_pre"] = np.ascontiguousarray(ctab[:, 0:n_pre])
            m["cmask"] = np.zeros((128, 1), np.float32)
        in_maps.append(m)
    return in_maps


def kernel(**inp):
    in_maps = make_in_maps(inp)
    nc = build_nc()
    res = run_bass_kernel_spmd(nc, in_maps, core_ids=list(range(8)))
    out = np.empty((4, 8192, 1024), np.float32)
    for core in range(8):
        b, j = core // 2, core % 2
        out[b, j * 4096:(j + 1) * 4096, :] = res.results[core]["out_t"].T
    return out
